# version 03 (final)
# baseline (speedup 1.0000x reference)
import math
import numpy as np
import ml_dtypes

import concourse.bass as bass
import concourse.mybir as mybir
from concourse.bass_utils import run_bass_kernel_spmd

F32 = mybir.dt.float32
BF16 = mybir.dt.bfloat16
FP8 = mybir.dt.float8e4
AF = mybir.ActivationFunctionType
ALU = mybir.AluOpType
AX = mybir.AxisListType
BF = ml_dtypes.bfloat16

D = 2048
S = 2048
NQ = 1024
NH = 16
HID = 8192
NJC = D // 128
INV_D = 1.0 / D
LN_EPS = 1e-5
SM_SCALE = 1.0 / math.sqrt(128.0)
EXP_SHIFT = -8.0

SBUF_BASE = 16384 + 2048
SBUF_TOP = 224 * 1024 - 64

HC = 128
NCHUNK = HID // HC
RSLOTS = 8
NROUND = NCHUNK // RSLOTS


class Em:
    ENGS = ("tensor", "scalar", "vector", "sync", "gpsimd")

    def __init__(self):
        self.q = {e: [] for e in self.ENGS}
        self.cnt = {}
        self.step = {}
        self.handles = {}

    def sem(self, name, step):
        self.cnt[name] = 0
        self.step[name] = step

    def wait(self, eng, name, val):
        if val is None or val <= 0:
            return
        assert val <= self.cnt[name], (eng, name, val, self.cnt[name])
        self.q[eng].append(lambda e, n=name, v=val: e.wait_ge(self.handles[n], v))

    def do(self, eng, fn, inc=None):
        if inc is None:
            self.q[eng].append(fn)
            return None
        self.cnt[inc] += self.step[inc]
        v = self.cnt[inc]
        self.q[eng].append(
            lambda e, f=fn, n=inc: f(e).then_inc(self.handles[n], self.step[n]))
        return v


def causal_units(qh):
    units = []
    lo, hi = qh * 512, qh * 512 + 512
    for kt in range(16):
        q0s = 0 if kt < 8 else (kt - 8) * 128
        qo = max(q0s, lo)
        w = hi - qo
        if w <= 0:
            continue
        diag = q0s if (kt >= 8 and lo <= q0s < hi) else None
        units.append((kt, qo, w, diag))
    return units


def build_program(taps=(), stop_after=None):
    nc = bass.Bass("TRN2", target_bir_lowering=False, debug=False,
                   detect_race_conditions=False)

    x_d = nc.dram_tensor("x", [S, D], F32, kind="ExternalInput").ap()
    wqt_d = nc.dram_tensor("wqt", [D, D], F32, kind="ExternalInput").ap()
    wkt_d = nc.dram_tensor("wkt", [D, D], F32, kind="ExternalInput").ap()
    wvt_d = nc.dram_tensor("wvt", [D, D], F32, kind="ExternalInput").ap()
    wot_d = nc.dram_tensor("wot", [D, D], F32, kind="ExternalInput").ap()
    wg0_d = nc.dram_tensor("wg0", [D, HID], F32, kind="ExternalInput").ap()
    wg1_d = nc.dram_tensor("wg1", [D, HID], F32, kind="ExternalInput").ap()
    wl_d = nc.dram_tensor("wl", [HID, D], F32, kind="ExternalInput").ap()
    cf32_d = nc.dram_tensor("cf32", [128, 100], F32, kind="ExternalInput").ap()
    cbf_d = nc.dram_tensor("cbf", [128, 384], BF16, kind="ExternalInput").ap()
    out_d = nc.dram_tensor("out", [NQ, D], F32, kind="ExternalOutput").ap()

    tap_shapes = {"lnxT": ([128, NJC * S], BF16), "qT": ([128, NH * NQ], BF16),
                  "attnT": ([128, NH * NQ], BF16), "x2": ([128, 8 * D], F32),
                  "lnx2T": ([128, NJC * NQ], BF16), "kvh0": ([128, S], BF16),
                  "vth0": ([128, S], BF16), "drecip": ([128, NQ], F32)}
    tap_d = {t: nc.dram_tensor("tap_" + t, tap_shapes[t][0], tap_shapes[t][1],
                               kind="ExternalOutput").ap() for t in taps}

    off = [SBUF_BASE]

    def alloc(name, shape, dtype, at=None):
        nbytes = int(np.prod(shape[1:])) * mybir.dt.size(dtype)
        o = off[0] if at is None else at
        h = nc.alloc_sbuf_tensor_at(name, shape, dtype, offset=o)
        end = o + nbytes
        assert end <= SBUF_TOP, (name, o, nbytes)
        if at is None:
            off[0] = o + ((nbytes + 31) // 32) * 32
        return h

    cf32 = alloc("cf32s", [128, 100], F32)
    identb_t = alloc("identb", [128, 128], BF16)
    trimask_t = alloc("trimask", [128, 128], BF16)
    onesb_t = alloc("onesb", [128, 128], BF16)
    stat = alloc("stat", [128, 16], F32)
    statx = alloc("statx", [128, 4], F32)
    sdum_c = alloc("sdum", [128, 1], F32)
    base_end = off[0]

    REG_P = base_end
    REG_Q = REG_P + 65536
    REG_R = REG_Q + 32768
    REG_S2 = REG_R + 32768

    lnxT = alloc("lnxT", [128, NJC, S], BF16, at=REG_P)
    x2 = alloc("x2", [128, 8, D], F32, at=REG_P)
    qT = alloc("qT", [128, NH, NQ], BF16, at=REG_Q)
    lnx2T = alloc("lnx2T", [128, NJC, NQ], FP8, at=REG_Q)
    attnT = alloc("attnT", [128, NH, NQ], BF16, at=REG_R)
    wlbufA = alloc("wlbufA", [128, RSLOTS, D], FP8, at=REG_R)

    s = REG_S2
    kvh = [alloc(f"kvh{i}", [128, S], BF16, at=s + i * 4096) for i in range(2)]
    s += 8192
    vtok = [alloc(f"vtok{i}", [128, 16, 128], BF16, at=s + i * 4096) for i in range(2)]
    s += 8192
    vth = [alloc(f"vth{i}", [128, S], BF16, at=s + i * 4096) for i in range(2)]
    s += 8192
    pbuf = [alloc(f"pbuf{i}", [128, NQ], BF16, at=s + i * 2048) for i in range(2)]
    s += 4096
    wb = [alloc(f"wb{i}", [128, NJC, 128], BF16, at=s + i * 4096) for i in range(4)]
    s += 16384
    drecip = alloc("drecip", [128, NQ], F32, at=s)
    s += 4096
    xstage = [alloc(f"xstage{i}", [128, D], F32, at=s + i * 8192) for i in range(2)]
    s += 16384
    sqscr = alloc("sqscr", [128, D], BF16, at=s)
    s += 4096
    lnstage = [alloc(f"lnstage{i}", [128, D], BF16, at=s + i * 4096) for i in range(2)]
    s += 8192
    assert s <= SBUF_TOP, s

    s = REG_S2
    wlbufB = alloc("wlbufB", [128, RSLOTS, D], FP8, at=s)
    s += 32768
    wgb = [alloc(f"wgb{i}", [128, NJC, HC], FP8, at=s + i * 4096) for i in range(4)]
    s += 16384
    agrp = alloc("agrp", [128, RSLOTS, NQ], FP8, at=s)
    s += 16384
    gbuf = [alloc(f"gbuf{i}", [128, NQ], BF16, at=s + i * 2048) for i in range(2)]
    s += 4096
    hbuf = [alloc(f"hbuf{i}", [128, NQ], BF16, at=s + i * 2048) for i in range(2)]
    s += 4096
    assert s <= SBUF_TOP, s

    resT = kvh[0]
    wlbuf = [wlbufA, wlbufB]

    pbk = [nc.alloc_psum_tensor(f"pb{i}", [128, 512], F32) for i in range(4)]
    pbq = nc.alloc_psum_tensor("pbq", [128, 2048], F32)

    identb = identb_t[:, :]
    trimask = trimask_t[:, :]
    onesb = onesb_t[:, :]
    bq_c = lambda i: cf32[:, 0 + i:1 + i]
    bk_c = lambda i: cf32[:, 16 + i:17 + i]
    bv_c = lambda i: cf32[:, 32 + i:33 + i]
    bo_c = lambda i: cf32[:, 48 + i:49 + i]
    emsc_c = lambda kt: cf32[:, 64 + kt:65 + kt]
    embi_c = lambda kt: cf32[:, 80 + kt:81 + kt]
    eps_c = cf32[:, 96:97]

    em = Em()
    em.sem("T", 1)
    em.sem("S", 1)
    em.sem("V", 1)
    for i in range(10):
        em.sem(f"G{i}", 16)
    em.sem("Y0", 16)
    em.sem("Y1", 16)
    em.sem("TP", 16)
    em.sem("C", 16)
    em.sem("X2", 16)
    em.sem("O", 16)

    bank_consumer = {}
    buf_consumer = {}

    def twait_bank(eng, tag):
        c = bank_consumer.get(tag)
        if c is not None:
            em.wait(eng, *c)

    def prodwait(eng, buf):
        c = buf_consumer.get(buf)
        if c is None:
            return
        if isinstance(c, list):
            for cc in c:
                em.wait(eng, *cc)
        else:
            em.wait(eng, *c)

    def emit_tap(t, waits):
        if t not in tap_d:
            return 0
        src_h, pat = {
            "lnxT": (lnxT, "p a b -> p (a b)"), "qT": (qT, "p a b -> p (a b)"),
            "attnT": (attnT, "p a b -> p (a b)"), "x2": (x2, "p a b -> p (a b)"),
            "lnx2T": (lnx2T, "p a b -> p (a b)"), "kvh0": (kvh[0], None),
            "vth0": (vth[0], None), "drecip": (drecip, None)}[t]
        for eng_sem in waits:
            em.wait("sync", *eng_sem)
        ap = src_h[:, :, :].rearrange(pat) if pat else src_h[:, :]
        return em.do("sync", lambda e, t=t, ap=ap: e.dma_start(
            out=tap_d[t], in_=ap), inc="TP")

    em.do("scalar", lambda e: e.dma_start(out=cf32[:, :], in_=cf32_d), inc="C")
    em.do("scalar", lambda e: e.dma_start(out=identb, in_=cbf_d[:, 0:128]), inc="C")
    em.do("scalar", lambda e: e.dma_start(out=trimask, in_=cbf_d[:, 128:256]), inc="C")
    cc = em.do("scalar", lambda e: e.dma_start(out=onesb, in_=cbf_d[:, 256:384]), inc="C")
    const_ad = ("C", cc)

    def emit_ln_tile(tt, src_ap, dstT, dst_col):
        g = tt % 2
        c_sum = stat[:, 0 + g:1 + g]
        c_ssq = stat[:, 2 + g:3 + g]
        c_negm = stat[:, 4 + g:5 + g]
        c_ssn = stat[:, 6 + g:7 + g]
        c_var = stat[:, 8 + g:9 + g]
        c_sd = stat[:, 10 + g:11 + g]
        c_nb = stat[:, 12 + g:13 + g]
        c_mean = statx[:, 0 + g:1 + g]
        vdum = lambda e: e.tensor_scalar_mul(statx[:, 2:3], c_sum, 1.0)
        em.do("vector", lambda e: e.reduce_sum(c_sum, src_ap, axis=AX.X))
        prodwait("scalar", f"stat{g}")
        sc = em.do("scalar", lambda e: e.activation(
            sqscr[:, :], src_ap, AF.Square, accum_out=c_ssq), inc="S")
        em.wait("vector", "S", sc)
        em.do("vector", lambda e: e.tensor_scalar_mul(c_mean, c_sum, INV_D))
        em.do("vector", lambda e: e.tensor_scalar_mul(c_negm, c_sum, -INV_D))
        em.do("vector", lambda e: e.tensor_scalar_mul(c_ssn, c_ssq, INV_D))
        em.do("vector", lambda e: e.tensor_mul(c_var, c_mean, c_mean))
        em.do("vector", vdum)
        v = em.do("vector", lambda e: e.tensor_sub(c_var, c_ssn, c_var), inc="V")
        buf_consumer[f"stat{g}"] = ("V", v)
        em.wait("scalar", "V", v)
        em.wait("scalar", *const_ad)
        sc = em.do("scalar", lambda e: e.activation(
            c_sd, c_var, AF.Sqrt, bias=eps_c), inc="S")
        em.wait("vector", "S", sc)
        em.do("vector", lambda e: e.reciprocal(c_sd, c_sd))
        em.do("vector", vdum)
        v = em.do("vector", lambda e: e.tensor_mul(c_nb, c_negm, c_sd), inc="V")
        em.wait("scalar", "V", v)
        prodwait("scalar", f"lnstage{g}")
        normc = em.do("scalar", lambda e: e.activation(
            lnstage[g][:, :], src_ap, AF.Identity, bias=c_nb, scale=c_sd), inc="S")
        lastv = None
        for b4 in range(4):
            tag = "b2" if b4 % 2 == 0 else "b3"
            bank = pbk[2 + (b4 % 2)]
            twait_bank("tensor", tag)
            em.wait("tensor", "S", normc)
            em.wait("tensor", *const_ad)
            for j in range(4):
                jj = b4 * 4 + j
                tc = em.do("tensor", lambda e, jj=jj, j=j, bank=bank, g=g: e.matmul(
                    bank[:, j * 128:(j + 1) * 128],
                    lnstage[g][:, jj * 128:(jj + 1) * 128],
                    identb, start=True, stop=True),
                    inc=("T" if j == 3 else None))
            em.wait("vector", "T", tc)
            lastv = em.do("vector", lambda e, b4=b4, bank=bank, dst_col=dst_col, dstT=dstT: e.tensor_copy(
                dstT[:, b4 * 4:(b4 + 1) * 4, dst_col:dst_col + 128],
                bank[:, :].rearrange("p (a b) -> p a b", a=4)), inc="V")
            bank_consumer[tag] = ("V", lastv)
        buf_consumer[f"lnstage{g}"] = ("T", tc)
        return lastv, normc

    xload = {}

    def emit_xload(tt):
        prodwait("sync", f"xstage{tt % 2}")
        v = em.do("sync", lambda e, tt=tt: e.dma_start(
            out=xstage[tt % 2][:, :], in_=x_d[tt * 128:(tt + 1) * 128, :]),
            inc=f"Y{tt % 2}")
        xload[tt] = (f"Y{tt % 2}", v)

    emit_xload(0)
    emit_xload(1)
    lnxT_done = None
    for tt in range(16):
        em.wait("vector", *xload[tt])
        em.wait("scalar", *xload[tt])
        lnxT_done, normc = emit_ln_tile(tt, xstage[tt % 2][:, :], lnxT, tt * 128)
        buf_consumer[f"xstage{tt % 2}"] = ("S", normc)
        if tt + 2 < 16:
            emit_xload(tt + 2)

    tp_lnxT = emit_tap("lnxT", [("V", lnxT_done)])
    if stop_after == "ln1":
        em.wait("scalar", "TP", em.cnt["TP"])
        return _finish(nc, em, ctx_close=None)

    wq_dma = {}

    def emit_wq_dma(it):
        prodwait("gpsimd", f"wb{it % 2}")
        v = em.do("gpsimd", lambda e, it=it: e.dma_start(
            out=wb[it % 2][:, :, :],
            in_=wqt_d[:, it * 128:(it + 1) * 128].rearrange("(jc p) i -> p jc i", p=128)),
            inc=f"G{it % 2}")
        wq_dma[it] = (f"G{it % 2}", v)

    emit_wq_dma(0)
    emit_wq_dma(1)
    q_evac = {}
    for it in range(16):
        em.wait("tensor", *wq_dma[it])
        if it == 0:
            em.wait("tensor", "V", lnxT_done)
        for qc in range(2):
            tag = f"b{qc}"
            twait_bank("tensor", tag)
            for jc in range(NJC):
                tc = em.do("tensor", lambda e, it=it, qc=qc, jc=jc: e.matmul(
                    pbk[qc][:, :], wb[it % 2][:, jc, :],
                    lnxT[:, jc, 1024 + qc * 512:1024 + qc * 512 + 512],
                    start=(jc == 0), stop=(jc == NJC - 1)),
                    inc=("T" if jc == NJC - 1 else None))
            em.wait("scalar", "T", tc)
            scq = em.do("scalar", lambda e, it=it, qc=qc: e.activation(
                qT[:, it, qc * 512:qc * 512 + 512], pbk[qc][:, :],
                AF.Identity, bias=bq_c(it), scale=1.0), inc="S")
            bank_consumer[tag] = ("S", scq)
            q_evac[(it, qc)] = scq
        buf_consumer[f"wb{it % 2}"] = ("T", tc)
        if it + 2 < 16:
            emit_wq_dma(it + 2)

    kv_dma = {}
    kv_ready = {}
    kv_prod_T = {}
    attn_evac = {}

    def emit_kv_dma(h):
        prodwait("gpsimd", f"wb{h % 2}")
        a = em.do("gpsimd", lambda e, h=h: e.dma_start(
            out=wb[h % 2][:, :, :],
            in_=wkt_d[:, h * 128:(h + 1) * 128].rearrange("(jc p) i -> p jc i", p=128)),
            inc=f"G{h % 2}")
        prodwait("gpsimd", f"wb{2 + h % 2}")
        b = em.do("gpsimd", lambda e, h=h: e.dma_start(
            out=wb[2 + h % 2][:, :, :],
            in_=wvt_d[:, h * 128:(h + 1) * 128].rearrange("(jc p) i -> p jc i", p=128)),
            inc=f"G{2 + h % 2}")
        kv_dma[h] = ((f"G{h % 2}", a), (f"G{2 + h % 2}", b))

    def emit_kv_production(h):
        em.wait("tensor", *kv_dma[h][0])
        em.wait("tensor", *kv_dma[h][1])
        if h == 0:
            em.wait("tensor", "V", lnxT_done)
        last_s = None
        for mat in range(2):
            wbuf = wb[h % 2] if mat == 0 else wb[2 + h % 2]
            dst = kvh[h % 2] if mat == 0 else vth[h % 2]
            bias = bk_c(h) if mat == 0 else bv_c(h)
            for kc in range(4):
                tag = f"b{4 + kc % 2}"
                bank = pbq[:, (kc % 2) * 512:(kc % 2) * 512 + 512]
                twait_bank("tensor", tag)
                for jc in range(NJC):
                    tc = em.do("tensor", lambda e, wbuf=wbuf, jc=jc, kc=kc, bank=bank: e.matmul(
                        bank, wbuf[:, jc, :],
                        lnxT[:, jc, kc * 512:(kc + 1) * 512],
                        start=(jc == 0), stop=(jc == NJC - 1)),
                        inc=("T" if jc == NJC - 1 else None))
                em.wait("scalar", "T", tc)
                if kc == 0:
                    prodwait("scalar", f"kvh{h % 2}" if mat == 0 else f"vth{h % 2}")
                last_s = em.do("scalar", lambda e, dst=dst, kc=kc, bias=bias, bank=bank: e.activation(
                    dst[:, kc * 512:(kc + 1) * 512], bank,
                    AF.Identity, bias=bias, scale=1.0), inc="S")
                bank_consumer[tag] = ("S", last_s)
        k_done = last_s
        vt_done = None
        for b4 in range(4):
            tag = f"b{4 + b4 % 2}"
            bank = pbq[:, (b4 % 2) * 512:(b4 % 2) * 512 + 512]
            twait_bank("tensor", tag)
            em.wait("tensor", "S", last_s)
            if b4 == 0:
                prodwait("vector", f"vtok{h % 2}")
            for j in range(4):
                kt = b4 * 4 + j
                tc = em.do("tensor", lambda e, kt=kt, j=j, bank=bank, h=h: e.matmul(
                    bank[:, j * 128:(j + 1) * 128],
                    vth[h % 2][:, kt * 128:(kt + 1) * 128],
                    identb, start=True, stop=True),
                    inc=("T" if j == 3 else None))
            em.wait("vector", "T", tc)
            vt_done = em.do("vector", lambda e, b4=b4, bank=bank, h=h: e.tensor_copy(
                vtok[h % 2][:, b4 * 4:(b4 + 1) * 4, :],
                bank.rearrange("p (a b) -> p a b", a=4)), inc="V")
            bank_consumer[tag] = ("V", vt_done)
        kv_prod_T[h] = tc
        buf_consumer[f"vth{h % 2}"] = ("T", tc)
        kv_ready[h] = (k_done, vt_done)

    def emit_attention(h):
        em.wait("tensor", "S", kv_ready[h][0])
        em.wait("tensor", "V", kv_ready[h][1])
        last_ts = None
        last_td = None
        for qh in range(2):
            em.wait("tensor", "S", q_evac[(h, qh)])
            units = causal_units(qh)
            nu = len(units)
            exp_dep = {}

            def emit_scores(idx):
                kt, qo, w, diag = units[idx]
                lo = qo - qh * 512
                sb = pbk[idx % 2]
                tag = f"b{idx % 2}"
                twait_bank("tensor", tag)
                ts = em.do("tensor", lambda e, h=h, kt=kt, qo=qo, w=w, lo=lo, sb=sb: e.matmul(
                    sb[:, lo:lo + w], kvh[h % 2][:, kt * 128:(kt + 1) * 128],
                    qT[:, h, qo:qo + w], start=True, stop=True), inc="T")
                em.wait("scalar", "T", ts)
                prodwait("scalar", f"pbuf{idx % 2}")
                se = em.do("scalar", lambda e, idx=idx, kt=kt, qo=qo, w=w, lo=lo, sb=sb: e.activation(
                    pbuf[idx % 2][:, qo:qo + w], sb[:, lo:lo + w], AF.Exp,
                    bias=embi_c(kt), scale=emsc_c(kt)), inc="S")
                bank_consumer[f"b{idx % 2}"] = ("S", se)
                if diag is not None:
                    em.wait("vector", "S", se)
                    sv = em.do("vector", lambda e, idx=idx, diag=diag: e.tensor_mul(
                        pbuf[idx % 2][:, diag:diag + 128],
                        pbuf[idx % 2][:, diag:diag + 128], trimask), inc="V")
                    exp_dep[idx] = ("V", sv)
                else:
                    exp_dep[idx] = ("S", se)
                return ts

            def emit_avd(idx):
                kt, qo, w, diag = units[idx]
                lo = qo - qh * 512
                em.wait("tensor", *exp_dep[idx])
                if idx == 0:
                    twait_bank("tensor", "b2")
                    twait_bank("tensor", "b3")
                em.do("tensor", lambda e, h=h, kt=kt, qo=qo, w=w, lo=lo, idx=idx, nu=nu: e.matmul(
                    pbk[2][:, lo:lo + w], vtok[h % 2][:, kt, :],
                    pbuf[idx % 2][:, qo:qo + w],
                    start=(idx == 0), stop=(idx == nu - 1)))
                td = em.do("tensor", lambda e, kt=kt, qo=qo, w=w, lo=lo, idx=idx, nu=nu: e.matmul(
                    pbk[3][:, lo:lo + w], onesb,
                    pbuf[idx % 2][:, qo:qo + w],
                    start=(idx == 0), stop=(idx == nu - 1)), inc="T")
                buf_consumer[f"pbuf{idx % 2}"] = ("T", td)
                return td

            last_ts = emit_scores(0)
            for idx in range(1, nu):
                last_ts = emit_scores(idx)
                last_td = emit_avd(idx - 1)
            last_td = emit_avd(nu - 1)

            em.wait("scalar", "T", last_td)
            prodwait("scalar", "drecip")
            em.do("scalar", lambda e, qh=qh: e.activation(
                drecip[:, qh * 512:qh * 512 + 512], pbk[3][:, :], AF.Ln))
            em.do("scalar", lambda e: e.activation(sdum_c[:, 0:1], eps_c, AF.Identity))
            sr = em.do("scalar", lambda e, qh=qh: e.activation(
                drecip[:, qh * 512:qh * 512 + 512],
                drecip[:, qh * 512:qh * 512 + 512], AF.Exp, scale=-1.0), inc="S")
            bank_consumer["b3"] = ("S", sr)
            em.wait("vector", "S", sr)
            em.wait("vector", "T", last_td)
            ov = em.do("vector", lambda e, h=h, qh=qh: e.tensor_tensor(
                attnT[:, h, qh * 512:qh * 512 + 512], pbk[2][:, :],
                drecip[:, qh * 512:qh * 512 + 512], ALU.mult), inc="V")
            bank_consumer["b2"] = ("V", ov)
            buf_consumer["drecip"] = ("V", ov)
            attn_evac[(h, qh)] = ov
        buf_consumer[f"kvh{h % 2}"] = ("T", last_ts)
        buf_consumer[f"vtok{h % 2}"] = ("T", last_td)
        buf_consumer[f"wb{h % 2}"] = ("T", kv_prod_T[h])
        buf_consumer[f"wb{2 + h % 2}"] = ("T", kv_prod_T[h])

    tp_qT = emit_tap("qT", [("S", q_evac[(15, 1)])])
    if stop_after == "q":
        em.wait("scalar", "TP", em.cnt["TP"])
        return _finish(nc, em, ctx_close=None)
    emit_kv_dma(0)
    emit_kv_production(0)
    tp_kv = emit_tap("kvh0", [("S", kv_ready[0][0])])
    tp_kv = max(tp_kv, emit_tap("vth0", [("S", kv_ready[0][0])]))
    if tp_kv:
        buf_consumer["kvh0"] = [buf_consumer.get("kvh0") or ("V", 0), ("TP", tp_kv)]
        buf_consumer["vth0"] = [buf_consumer.get("vth0") or ("V", 0), ("TP", tp_kv)]
    for h in range(16):
        if h + 1 < 16:
            emit_kv_dma(h + 1)
            emit_kv_production(h + 1)
        emit_attention(h)
    attn_all_done = attn_evac[(15, 1)]
    tp_attnT = emit_tap("attnT", [("V", attn_all_done)])
    tp_attnT = max(tp_attnT, emit_tap("drecip", [("V", attn_all_done)]))
    if stop_after == "heads":
        em.wait("scalar", "TP", em.cnt["TP"])
        return _finish(nc, em, ctx_close=None)

    x2_load = {}
    for tt in range(8):
        if tt == 0:
            em.wait("scalar", "T", kv_prod_T[15])
            em.wait("scalar", "TP", tp_lnxT)
        x2_load[tt] = em.do("scalar", lambda e, tt=tt: e.dma_start(
            out=x2[:, tt, :], in_=x_d[1024 + tt * 128:1024 + (tt + 1) * 128, :]),
            inc="X2")

    wo_dma = {}

    def emit_wo_dma(it):
        prodwait("gpsimd", f"wb{it % 2}")
        v = em.do("gpsimd", lambda e, it=it: e.dma_start(
            out=wb[it % 2][:, :, :],
            in_=wot_d[:, it * 128:(it + 1) * 128].rearrange("(jc p) i -> p jc i", p=128)),
            inc=f"G{it % 2}")
        wo_dma[it] = (f"G{it % 2}", v)

    emit_wo_dma(0)
    emit_wo_dma(1)
    wo_last_chain = None
    wo_done = None
    for it in range(16):
        em.wait("tensor", *wo_dma[it])
        if it == 0:
            em.wait("tensor", "V", attn_all_done)
        for qc in range(2):
            tag = f"b{qc}"
            twait_bank("tensor", tag)
            for jc in range(NJC):
                tc = em.do("tensor", lambda e, it=it, qc=qc, jc=jc: e.matmul(
                    pbk[qc][:, :], wb[it % 2][:, jc, :],
                    attnT[:, jc, qc * 512:qc * 512 + 512],
                    start=(jc == 0), stop=(jc == NJC - 1)),
                    inc=("T" if jc == NJC - 1 else None))
            em.wait("scalar", "T", tc)
            if qc == 0:
                prodwait("scalar", "resT")
            scw = em.do("scalar", lambda e, it=it, qc=qc: e.activation(
                resT[:, qc * 512:qc * 512 + 512], pbk[qc][:, :],
                AF.Identity, bias=bo_c(it), scale=1.0), inc="S")
            bank_consumer[tag] = ("S", scw)
        wo_last_chain = tc
        buf_consumer[f"wb{it % 2}"] = ("T", tc)
        if it + 2 < 16:
            emit_wo_dma(it + 2)
        for b4 in range(2):
            tag = f"b{2 + b4}"
            bank = pbk[2 + b4]
            twait_bank("tensor", tag)
            em.wait("tensor", "S", scw)
            for j in range(4):
                qt = b4 * 4 + j
                tc2 = em.do("tensor", lambda e, qt=qt, j=j, bank=bank: e.matmul(
                    bank[:, j * 128:(j + 1) * 128],
                    resT[:, qt * 128:(qt + 1) * 128],
                    identb, start=True, stop=True),
                    inc=("T" if j == 3 else None))
            em.wait("vector", "T", tc2)
            if it == 0:
                em.wait("vector", "X2", x2_load[7])
            va = em.do("vector", lambda e, it=it, b4=b4, bank=bank: e.tensor_tensor(
                x2[:, b4 * 4:(b4 + 1) * 4, it * 128:(it + 1) * 128],
                bank.rearrange("p (a b) -> p a b", a=4),
                x2[:, b4 * 4:(b4 + 1) * 4, it * 128:(it + 1) * 128],
                ALU.add), inc="V")
            bank_consumer[tag] = ("V", va)
            wo_done = va
        buf_consumer["resT"] = ("T", tc2)
        wo_tr_last = tc2

    tp_x2 = emit_tap("x2", [("V", wo_done)])
    if stop_after == "wo":
        em.wait("scalar", "TP", em.cnt["TP"])
        return _finish(nc, em, ctx_close=None)

    em.wait("vector", "V", wo_done)
    em.wait("scalar", "V", wo_done)
    em.wait("vector", "TP", tp_qT)
    lnx2_done = None
    for tt in range(8):
        lnx2_done, _ = emit_ln_tile(tt, x2[:, tt, :], lnx2T, tt * 128)

    if stop_after == "ln2":
        emit_tap("lnx2T", [("V", lnx2_done)])
        em.wait("scalar", "TP", em.cnt["TP"])
        return _finish(nc, em, ctx_close=None)

    era2_gate = [("T", wo_tr_last), ("V", attn_all_done), ("TP", tp_attnT)]
    for n in ("wl0", "wl1", "wgb0", "wgb1", "wgb2", "wgb3"):
        buf_consumer[n] = list(era2_gate)

    wg_dma = {}
    wl_dma = {}

    def emit_wl_dmas(r):
        for ht in range(RSLOTS):
            prodwait("gpsimd", f"wl{r % 2}")
            v = em.do("gpsimd", lambda e, r=r, ht=ht: e.dma_start(
                out=wlbuf[r % 2][:, ht, :],
                in_=wl_d[r * 1024 + ht * 128:r * 1024 + (ht + 1) * 128, :]),
                inc=f"G{8 + r % 2}")
            wl_dma[(r, ht)] = (f"G{8 + r % 2}", v)

    def emit_wg_dmas(c):
        for g in range(2):
            bufn = f"wgb{2 * g + c % 2}"
            prodwait("gpsimd", bufn)
            src = wg0_d if g == 0 else wg1_d
            v = em.do("gpsimd", lambda e, c=c, g=g, src=src: e.dma_start(
                out=wgb[2 * g + c % 2][:, :, :],
                in_=src[:, c * HC:(c + 1) * HC].rearrange("(jc p) h -> p jc h", p=128)),
                inc=f"G{4 + 2 * g + c % 2}")
            wg_dma[(c, g)] = (f"G{4 + 2 * g + c % 2}", v)

    emit_wl_dmas(0)
    emit_wg_dmas(0)
    emit_wg_dmas(1)

    amul_cnt = {}
    final_add = {}
    for r in range(NROUND):
        if r + 1 < NROUND:
            emit_wl_dmas(r + 1)
        for slot in range(RSLOTS):
            c = r * RSLOTS + slot
            scf = None
            for g in range(2):
                em.wait("tensor", *wg_dma[(c, g)])
                if c == 0 and g == 0:
                    em.wait("tensor", "V", lnx2_done)
                dstb = gbuf[c % 2] if g == 0 else hbuf[c % 2]
                func = AF.Gelu_apprx_tanh if g == 0 else AF.Identity
                for qc in range(2):
                    tag = f"b{2 * g + qc}"
                    twait_bank("tensor", tag)
                    for jc in range(NJC):
                        tc = em.do("tensor", lambda e, c=c, g=g, qc=qc, jc=jc: e.matmul(
                            pbk[2 * g + qc][:, :], wgb[2 * g + c % 2][:, jc, :],
                            lnx2T[:, jc, qc * 512:qc * 512 + 512],
                            start=(jc == 0), stop=(jc == NJC - 1)),
                            inc=("T" if jc == NJC - 1 else None))
                    em.wait("scalar", "T", tc)
                    if g == 0 and qc == 0:
                        prodwait("scalar", f"gh{c % 2}")
                    scf = em.do("scalar", lambda e, dstb=dstb, g=g, qc=qc, func=func: e.activation(
                        dstb[:, qc * 512:qc * 512 + 512], pbk[2 * g + qc][:, :], func),
                        inc="S")
                    bank_consumer[tag] = ("S", scf)
                buf_consumer[f"wgb{2 * g + c % 2}"] = ("T", tc)
            if c + 2 < NCHUNK:
                emit_wg_dmas(c + 2)
            em.wait("vector", "S", scf)
            av = em.do("vector", lambda e, c=c, slot=slot: e.tensor_mul(
                agrp[:, slot, :], gbuf[c % 2][:, :], hbuf[c % 2][:, :]), inc="V")
            amul_cnt[(r, slot)] = av
            buf_consumer[f"gh{c % 2}"] = ("V", av)
        em.wait("tensor", "V", amul_cnt[(r, RSLOTS - 1)])
        em.wait("tensor", *wl_dma[(r, RSLOTS - 1)])
        if r == 0:
            twait_bank("tensor", "b4")
            twait_bank("tensor", "b5")
        for tt in range(8):
            twait_bank("tensor", "bq")
            for ht in range(RSLOTS):
                for ib in range(4):
                    tc = em.do("tensor", lambda e, r=r, tt=tt, ht=ht, ib=ib: e.matmul(
                        pbq[:, ib * 512:(ib + 1) * 512],
                        agrp[:, ht, tt * 128:(tt + 1) * 128],
                        wlbuf[r % 2][:, ht, ib * 512:(ib + 1) * 512],
                        start=(ht == 0), stop=(ht == RSLOTS - 1)),
                        inc=("T" if (ht == RSLOTS - 1 and ib == 3) else None))
            em.wait("vector", "T", tc)
            if r == 0 and tt == 0:
                em.wait("vector", "TP", tp_x2)
            va = em.do("vector", lambda e, r=r, tt=tt: e.tensor_tensor(
                x2[:, tt, :], pbq[:, :], x2[:, tt, :], ALU.add), inc="V")
            bank_consumer["bq"] = ("V", va)
            if r == NROUND - 1:
                final_add[tt] = va
        buf_consumer[f"wl{r % 2}"] = ("T", tc)

    for tt in range(8):
        em.wait("scalar", "V", final_add[tt])
        em.do("scalar", lambda e, tt=tt: e.dma_start(
            out=out_d[tt * 128:(tt + 1) * 128, :], in_=x2[:, tt, :]), inc="O")

    emit_tap("lnx2T", [("V", lnx2_done)])

    for n in list(em.cnt):
        if n not in ("T", "S", "V") and em.cnt[n]:
            em.wait("scalar", n, em.cnt[n])

    return _finish(nc, em, ctx_close=None)


def _finish(nc, em, ctx_close=None):
    from contextlib import ExitStack
    ctx = ExitStack()
    for name in em.cnt:
        em.handles[name] = ctx.enter_context(nc.semaphore(name))
    with nc.Block() as block:
        block.tensor(lambda e: [f(e) for f in em.q["tensor"]])
        block.scalar(lambda e: [f(e) for f in em.q["scalar"]])
        block.vector(lambda e: [f(e) for f in em.q["vector"]])
        block.sync(lambda e: [f(e) for f in em.q["sync"]])
        block.gpsimd(lambda e: [f(e) for f in em.q["gpsimd"]])
    ctx.close()
    return nc


def host_constants(half):
    cf32 = np.zeros((128, 100), dtype=np.float32)
    cbf = np.zeros((128, 384), dtype=BF)
    cbf[:, 0:128] = np.eye(128, dtype=BF)
    tri = (np.arange(128)[None, :] >= np.arange(128)[:, None])
    cbf[:, 128:256] = tri.astype(BF)
    cbf[:, 256:384] = np.ones((128, 128), dtype=BF)
    for kt in range(16):
        phantom = (half == 0 and kt < 8)
        cf32[:, 64 + kt] = 0.0 if phantom else SM_SCALE
        cf32[:, 80 + kt] = -40.0 if phantom else EXP_SHIFT
    cf32[:, 96] = LN_EPS
    cf32[:, 97] = 1.0
    return cf32, cbf


_PROG_CACHE = {}


def kernel(L0_attn_X, L0_attn_Wq, L0_attn_bq, L0_attn_Wk, L0_attn_bk,
           L0_attn_Wv, L0_attn_bv, L0_attn_Wo, L0_attn_bo,
           L0_ffn_w_gating, L0_ffn_w_linear):
    X = np.asarray(L0_attn_X, dtype=np.float32)
    wqt = np.ascontiguousarray(np.asarray(L0_attn_Wq, np.float32).T)
    wkt = np.ascontiguousarray(np.asarray(L0_attn_Wk, np.float32).T)
    wvt = np.ascontiguousarray(np.asarray(L0_attn_Wv, np.float32).T)
    wot = np.ascontiguousarray(np.asarray(L0_attn_Wo, np.float32).T)
    wg = np.asarray(L0_ffn_w_gating, np.float32)
    wg0 = np.ascontiguousarray(wg[0])
    wg1 = np.ascontiguousarray(wg[1])
    wl = np.ascontiguousarray(np.asarray(L0_ffn_w_linear, np.float32))

    def bias_block(b):
        return np.ascontiguousarray(np.asarray(b, np.float32).reshape(16, 128).T)

    consts = {}
    for half in (0, 1):
        cf32, cbf = host_constants(half)
        cf32[:, 0:16] = bias_block(L0_attn_bq)
        cf32[:, 16:32] = bias_block(L0_attn_bk)
        cf32[:, 32:48] = bias_block(L0_attn_bv)
        cf32[:, 48:64] = bias_block(L0_attn_bo)
        consts[half] = (np.ascontiguousarray(cf32), np.ascontiguousarray(cbf))

    if "nc" not in _PROG_CACHE:
        _PROG_CACHE["nc"] = build_program()
    nc = _PROG_CACHE["nc"]

    in_maps = []
    for b in range(4):
        for half in (0, 1):
            if half == 1:
                x_in = np.ascontiguousarray(X[b])
            else:
                x_in = np.ascontiguousarray(
                    np.concatenate([X[b, :1024], X[b, :1024]], axis=0))
            cf32, cbf = consts[half]
            in_maps.append(dict(
                x=x_in, wqt=wqt, wkt=wkt, wvt=wvt, wot=wot,
                wg0=wg0, wg1=wg1, wl=wl, cf32=cf32, cbf=cbf))

    res = run_bass_kernel_spmd(nc, in_maps, core_ids=list(range(8)))

    out = np.empty((4, 2048, 2048), dtype=np.float32)
    for b in range(4):
        out[b, 0:1024] = res.results[2 * b]["out"]
        out[b, 1024:2048] = res.results[2 * b + 1]["out"]
    return out
